# version 09
# baseline (speedup 1.0000x reference)
import contextlib
import os
from collections import defaultdict

import numpy as np

import concourse.bass as bass
import concourse.mybir as mybir
from concourse.bass_utils import run_bass_kernel_spmd

F32 = mybir.dt.float32
BF16 = mybir.dt.bfloat16
AF = mybir.ActivationFunctionType
ALU = mybir.AluOpType

CORES = 8
L = 16384
DM = 2048
NSTATE = 16
E = 2080
SL = L // CORES
TT = 256
NSUB = SL // TT
NDB = DM // 128
NEB = 17
NGRP = 8
NGPD = NSTATE // NGRP
CH = TT + 2
GSUB = NDB * NGPD
LNEPS = 1e-5

ENGINES = ("sync", "scalar", "vector", "tensor", "gpsimd")


def build_nc(stage=None):
    if stage is None:
        stage = int(os.environ.get("KDBG_STAGE", "3"))
    nsub_eff = int(os.environ.get("KDBG_NSUB", str(NSUB)))
    p1cut = int(os.environ.get("KDBG_P1CUT", "3"))
    p2cut = int(os.environ.get("KDBG_P2CUT", "3"))
    k_xp = int(os.environ.get("KDBG_XP", "1"))
    k_sp = int(os.environ.get("KDBG_SP", "1"))
    k_cs = int(os.environ.get("KDBG_CS", "1"))
    k_bcl = int(os.environ.get("KDBG_BCL", "1"))
    k_spv = int(os.environ.get("KDBG_SPV", "1"))
    k_spn = int(os.environ.get("KDBG_SPN", "16"))
    k_spb = int(os.environ.get("KDBG_SPB", "1"))
    nc = bass.Bass("TRN2", target_bir_lowering=False, debug=False,
                   detect_race_conditions=False)

    xT = nc.dram_tensor("xT", [DM, SL], F32, kind="ExternalInput").ap()
    wxT = nc.dram_tensor("wxT", [DM, E], F32, kind="ExternalInput").ap()
    wfT = nc.dram_tensor("wfT", [DM, DM], F32, kind="ExternalInput").ap()
    alogr = nc.dram_tensor("alogr", [128, NDB * NSTATE], F32, kind="ExternalInput").ap()
    dvecr = nc.dram_tensor("dvecr", [128, NDB], F32, kind="ExternalInput").ap()
    fbias = nc.dram_tensor("fbias", [1, DM], F32, kind="ExternalInput").ap()
    eye_in = nc.dram_tensor("eye_in", [128, 128], BF16, kind="ExternalInput").ap()
    cmask = nc.dram_tensor("cmask", [128, CORES], F32, kind="ExternalInput").ap()
    y_out = nc.dram_tensor("y_out", [SL, DM], F32, kind="ExternalOutput").ap()

    P_scr = nc.dram_tensor("P_scr", [DM, SL], F32).ap()
    z_scr = nc.dram_tensor("z_scr", [DM, SL], F32).ap()
    bc_scr = nc.dram_tensor("bc_scr", [32, SL], F32).ap()
    st_scr = nc.dram_tensor("st_scr", [2, TT], F32).ap()
    al_scr = nc.dram_tensor("al_scr", [1, DM], F32).ap()
    cc_in = nc.dram_tensor("cc_in", [128, 512], F32).ap()
    cc_out = nc.dram_tensor("cc_out", [CORES * 128, 512], F32).ap()

    es = contextlib.ExitStack()
    with es:
        def sb(name, shape, dt=F32):
            return es.enter_context(nc.sbuf_tensor(name, shape, dt))

        slots = {}
        off = [0]

        def slot(name, cols):
            slots[name] = (off[0], cols)
            off[0] += cols

        slot("xt", NDB * TT)
        slot("dt", NDB * TT)
        slot("u", NDB * TT)
        slot("Bbc", NDB * TT)
        slot("Cbc", 2 * NGRP * CH)
        slot("a", 2 * NGRP * CH)
        slot("b", 2 * NGRP * CH)
        slot("h", NGRP * CH)
        slot("hC", NGRP * CH // 2)
        slot("xtb", NDB * TT // 2)
        slot("spl", 3 * TT)
        slot("ztr", 2 * TT)
        slot("ones", 1)
        slot("Ptr", 2 * TT)
        NCOL = off[0]
        AR = sb("AR", [128, NCOL], F32)

        def V(name, lo=0, n=None):
            o, c = slots[name]
            if n is None:
                n = c - lo
            return AR.ap()[:, o + lo: o + lo + n]

        def outfv():
            if int(os.environ.get("KDBG_OUTFU", "0")):
                return V("u", 0, DM)
            return V("b", 0, DM)

        def BCv():
            return V("ztr", 0, TT)[0:32, :]

        def onrbv():
            return V("b", 3072, 64).bitcast(BF16)[0:1, :]

        def fb16v():
            return V("b", 2048, 1024).bitcast(BF16)[0:1, :]

        def x2bv():
            return V("h", 0, DM).bitcast(BF16)

        def mubv():
            return V("xtb", 0, TT)

        def rbv():
            return V("xtb", TT, TT)

        def stv(i):
            if i == 0:
                return V("spl", 512, TT)[0:1, :]
            if i == 1:
                return V("spl", 0, TT)[0:1, :]
            return V("Ptr", TT, TT)[0:1, :]

        hin = V("Ptr", 0, TT)
        htmp = V("spl", 0, TT)
        Pseg = V("spl", TT, TT)
        hcar = sb("hcar", [128, NDB * NSTATE], F32)
        wunion = sb("wunion", [128, NDB * E], BF16)
        A_sb = sb("A_sb", [128, NDB * NSTATE], F32)
        D1_sb = sb("D1_sb", [128, NDB], F32)

        Pcar = sb("Pcar", [128, NDB], F32)
        mask_sb = sb("mask_sb", [128, CORES], F32)
        eyeb_sb = sb("eyeb_sb", [128, 128], BF16)
        onc_sb = sb("onc_sb", [128, 1], F32)
        eps_sb = sb("eps_sb", [128, 1], F32)
        c1_sb = sb("c1_sb", [128, 8], F32)
        xp_ps = [es.enter_context(nc.psum_tensor(f"xp{i}", [128, 512], F32)) for i in range(2)]
        y_ps = [es.enter_context(nc.psum_tensor(f"yp{i}", [128, 512], F32)) for i in range(2)]
        fn_ps = [es.enter_context(nc.psum_tensor(f"fn{i}", [128, 512], F32)) for i in range(2)]

        semnames = [
            "sX", "sWXL0", "sWXL1", "sCast", "sXP", "sSPa", "sSPv", "sSPl", "sDT", "sPC", "sPW0", "sPW1",
            "sBCW", "sBC", "sU", "sA", "sB", "sBB", "sH", "sHC", "sY", "sZst",
            "sZW0", "sZW1", "sWFL0", "sWFL1", "sABS", "sALM", "sALR", "sALW", "sALB", "sSGN", "sBW",
            "sPSG", "sCCW", "sCG", "sCCR", "sCMB", "sP2", "sZL", "sBC2", "sEA",
            "sECd", "sY2", "sZ2d", "sSQ", "sS2", "sST", "sSCa", "sSCv", "sRST", "sSTW",
            "sMUB", "sX2", "sTRE", "sFM", "sZ2t", "sR2", "sOW", "sIN", "sOF",
            "sCY", "sBI", "sSTR",
        ]
        SEM = {n: es.enter_context(nc.semaphore(n)) for n in semnames}

        PROG = {e: [] for e in ENGINES}
        CNT = defaultdict(int)

        def op(eng, fn, waits=(), inc=None, amt=1):
            for s_, v_ in waits:
                assert v_ <= CNT[s_], (eng, s_, v_, CNT[s_])
            if inc is not None:
                CNT[inc] += amt
            PROG[eng].append((fn, tuple((SEM[s_], v_) for s_, v_ in waits if v_ > 0),
                              SEM[inc] if inc else None, amt))
            return CNT[inc] if inc else None

        op("sync", lambda e: e.dma_start(out=eyeb_sb[:], in_=eye_in), inc="sIN", amt=16)
        op("sync", lambda e: e.dma_start(out=A_sb[:], in_=alogr), inc="sIN", amt=16)
        op("sync", lambda e: e.dma_start(out=D1_sb[:], in_=dvecr), inc="sIN", amt=16)
        op("sync", lambda e: e.dma_start(out=outfv()[0:1, :], in_=fbias), inc="sIN", amt=16)
        op("sync", lambda e: e.dma_start(out=mask_sb[:], in_=cmask), inc="sIN", amt=16)
        inval = CNT["sIN"]
        op("scalar", lambda e: e.activation(out=A_sb[:], in_=A_sb[:], func=AF.Exp),
           waits=[("sIN", inval)], inc="sSCa")
        op("vector", lambda e: e.tensor_scalar_mul(A_sb[:], A_sb[:], -1.0),
           waits=[("sSCa", CNT["sSCa"])], inc="sSCv")
        aready = CNT["sSCv"]
        op("vector", lambda e: e.tensor_scalar_add(D1_sb[:], D1_sb[:], 1.0),
           waits=[("sIN", inval)], inc="sSCv")
        d1ready = CNT["sSCv"]
        eybready = 0
        op("vector", lambda e: e.memset(onc_sb[:], 1.0), inc="sSCv")
        op("vector", lambda e: e.memset(eps_sb[:], LNEPS), inc="sSCv")
        op("vector", lambda e: e.memset(V("ones", 0, 1), 1.0), inc="sSCv")
        op("vector", lambda e: e.memset(Pcar[:], 0.0), inc="sSCv")
        pass_hin_init = None
        op("vector", lambda e: e.memset(
            V("a").rearrange("p (q c) -> p q c", c=CH)[:, :, 0:1], 0.0), inc="sSCv")
        op("vector", lambda e: e.memset(
            V("a").rearrange("p (q c) -> p q c", c=CH)[:, :, 1:2], 1.0), inc="sSCv")
        op("vector", lambda e: e.memset(
            V("b").rearrange("p (q c) -> p q c", c=CH)[:, :, 0:2], 0.0), inc="sSCv")
        op("vector", lambda e: e.memset(
            V("Cbc").rearrange("p (q c) -> p q c", c=CH)[:, :, 0:2], 0.0), inc="sSCv")
        constready = CNT["sSCv"]

        def wld(par):
            return V("Bbc", par * E, E)

        for db in range(NDB):
            par = db % 2
            w = [("sCast", CNT["sCast"] - 1)] if db >= 2 else []
            wxv = op("sync", lambda e, db=db, par=par: e.dma_start(
                out=wld(par), in_=wxT[db * 128:(db + 1) * 128, :]),
                waits=w, inc=f"sWXL{par}", amt=16)
            op("vector", lambda e, db=db, par=par: e.tensor_copy(
                wunion[:, db * E:(db + 1) * E], wld(par)),
                waits=[(f"sWXL{par}", wxv)], inc="sCast")
        wx_ready = CNT["sCast"]

        dt_ready = {}
        xp_free = {}
        pend_carry = []

        def flush_carry():
            while pend_carry:
                hval, cdb, cn0 = pend_carry.pop(0)
                op("scalar", lambda e, cdb=cdb, cn0=cn0: e.activation(
                    out=hcar[:, cdb * NSTATE + cn0: cdb * NSTATE + cn0 + NGRP],
                    in_=V("h", 0, NGRP * CH)
                    .rearrange("p (q c) -> p q c", c=CH)[:, :, CH - 1:CH].squeeze(2),
                    func=AF.Copy),
                    waits=[("sH", hval)], inc="sCY")

        pend_z = []

        def flush_z():
            while pend_z:
                pend_z.pop(0)()
        trans_done = {}
        z_count = 0

        def xtv(db):
            return V("xt", db * TT, TT)

        def dtv(db):
            return V("dt", db * TT, TT)

        def uv(db):
            return V("u", db * TT, TT)

        if stage == 0:
            op("gpsimd", lambda e: e.dma_start(out=y_out[0:128, 0:NDB * NSTATE], in_=A_sb[:]),
                waits=[("sSCv", constready), ("sCast", wx_ready)], inc="sOW", amt=16)
            op("gpsimd", lambda e: e.nop(), waits=[("sOW", 16)])
        for s in (range(nsub_eff) if stage >= 1 else []):
            flush_z()
            for db in range(NDB):
                w = [("sZst", (s - 1) * NDB + db + 1)] if s > 0 else []
                op("sync", lambda e, s=s, db=db: e.dma_start(
                    out=xtv(db),
                    in_=xT[db * 128:(db + 1) * 128, s * TT:(s + 1) * TT]),
                    waits=w, inc="sX", amt=16)
            w = [("sX", 16 * NDB * (s + 1))]
            if s > 0:
                w.append(("sXP", s * NEB))
            op("vector", lambda e: e.tensor_copy(V("xtb").bitcast(BF16), V("xt")),
               waits=w, inc="sCast")
            cast_val = CNT["sCast"]

            for eb in (range(min(NEB, int(os.environ.get("KDBG_NEB", str(NEB))))) if k_xp else []):
                par = eb % 2
                ecols = 128 if eb < 16 else 32
                for db in range(NDB):
                    w = []
                    if db == 0:
                        w.append(("sCast", cast_val))
                        if s == 0 and eb <= 1:
                            w.append(("sCast", wx_ready))
                        key = (s, eb - 2) if eb >= 2 else (s - 1, eb - 2 + NEB)
                        if key in xp_free:
                            w += xp_free[key]
                    op("tensor", lambda e, db=db, eb=eb, par=par, ecols=ecols: e.matmul(
                        out=xp_ps[par].ap()[0:ecols, 0:TT],
                        lhsT=wunion[:, db * E + eb * 128: db * E + eb * 128 + ecols],
                        rhs=V("xtb").bitcast(BF16)[:, db * TT:(db + 1) * TT],
                        start=(db == 0), stop=(db == NDB - 1)),
                        waits=w, inc=("sXP" if db == NDB - 1 else None))
                xpv = CNT["sXP"]
                if not k_sp or (eb < 16 and eb >= k_spn):
                    xp_free[(s, eb)] = []
                    continue
                if eb < 16:
                    sp0 = V("spl", par * TT, TT)
                    sp1 = V("spl", 2 * TT, TT)
                    w = [("sXP", xpv)]
                    if (s, eb - 2) in dt_ready:
                        w.append(("sDT", dt_ready[(s, eb - 2)]))
                    elif s > 0 and eb < 2:
                        w.append(("sDT", dt_ready[(s - 1, eb - 2 + 16)]))
                    op("scalar", lambda e, par=par, eb=eb: e.activation(
                        out=dtv(eb), in_=xp_ps[par].ap()[:, 0:TT], func=AF.Relu),
                        waits=w)
                    op("scalar", lambda e, par=par, sp0=sp0: e.activation(
                        out=sp0, in_=xp_ps[par].ap()[:, 0:TT], func=AF.Abs),
                        inc="sSPa")
                    spa1 = CNT["sSPa"]
                    op("scalar", lambda e, sp0=sp0, sp1=sp1: e.activation(
                        out=sp1, in_=sp0, func=AF.Exp, scale=-1.0))
                    op("scalar", lambda e, sp0=sp0, sp1=sp1: e.activation(
                        out=sp0, in_=sp1, func=AF.Ln, bias=1.0), inc="sSPl")
                    splv = CNT["sSPl"]
                    op("vector", lambda e, eb=eb, sp0=sp0: e.tensor_add(
                        dtv(eb), dtv(eb), sp0),
                        waits=[("sSPl", splv)], inc="sDT")
                    dt_ready[(s, eb)] = CNT["sDT"]
                    xp_free[(s, eb)] = [("sSPa", spa1), ("sSPl", splv)]
                    if not k_cs:
                        continue
                    db = eb
                    ppar = db % 2
                    w = [("sDT", dt_ready[(s, db)])]
                    if s * NDB + db >= 2:
                        w.append((f"sPW{(s * NDB + db) % 2}", CNT[f"sPW{(s * NDB + db) % 2}"]))
                    if s == 0 and db == 0:
                        w.append(("sSCv", constready))
                    op("vector", lambda e, db=db, ppar=ppar: e.tensor_tensor_scan(
                        out=V("Ptr", ppar * TT, TT),
                        data0=V("ones", 0, 1).broadcast_to((128, TT)),
                        data1=dtv(db),
                        initial=Pcar[:, db:db + 1], op0=ALU.mult, op1=ALU.add),
                        waits=w)
                    op("vector", lambda e, db=db, ppar=ppar: e.tensor_copy(
                        Pcar[:, db:db + 1], V("Ptr", ppar * TT + TT - 1, 1)), inc="sPC")
                    pc_val = CNT["sPC"]
                    op("sync", lambda e, db=db, ppar=ppar, s=s: e.dma_start(
                        out=P_scr[db * 128:(db + 1) * 128, s * TT:(s + 1) * TT],
                        in_=V("Ptr", ppar * TT, TT)),
                        waits=[("sPC", pc_val)], inc=f"sPW{(s * NDB + db) % 2}", amt=16)
                elif k_spb:
                    w = [("sXP", xpv)]
                    if s > 0:
                        w.append(("sBCW", 16 * s))
                    op("vector", lambda e, par=par: e.tensor_copy(
                        BCv(), xp_ps[par].ap()[0:32, 0:TT]),
                        waits=w, inc="sSPv")
                    bccp_val = CNT["sSPv"]
                    xp_free[(s, eb)] = [("sSPv", bccp_val)]
                    op("sync", lambda e, s=s: e.dma_start(
                        out=bc_scr[:, s * TT:(s + 1) * TT],
                        in_=BCv()),
                        waits=[("sSPv", bccp_val)], inc="sBCW", amt=16)
                else:
                    xp_free[(s, eb)] = []

            for n in (range(NSTATE) if k_bcl else []):
                w = [("sBCW", 16 * (s + 1))]
                if s > 0 and n == 0:
                    w += [("sB", s * GSUB), ("sHC", s * GSUB)]
                op("sync", lambda e, n=n, s=s: e.dma_start(
                    out=V("Bbc", n * TT, TT),
                    in_=bc_scr[n:n + 1, s * TT:(s + 1) * TT].broadcast_to((128, TT))),
                    waits=w, inc="sBC", amt=16)
                op("sync", lambda e, n=n, s=s: e.dma_start(
                    out=V("Cbc", (n // NGRP) * NGRP * CH + (n % NGRP) * CH + 2, TT),
                    in_=bc_scr[16 + n:17 + n, s * TT:(s + 1) * TT].broadcast_to((128, TT))),
                    inc="sBC", amt=16)
            bc_val = CNT["sBC"]

            if p1cut >= 2:
                for db in range(NDB):
                    w = [("sDT", dt_ready[(s, db)]), ("sX", 16 * NDB * (s + 1))]
                    if s > 0:
                        w.append(("sB", (s - 1) * GSUB + (db + 1) * NGPD))
                    op("gpsimd", lambda e, db=db: e.tensor_mul(uv(db), dtv(db), xtv(db)),
                       waits=w, inc="sU")

            for db in (range(NDB) if p1cut >= 2 else []):
                for gi in range(NGPD):
                    g = s * GSUB + db * NGPD + gi
                    gpar = g % 2
                    n0 = gi * NGRP
                    abase = gpar * NGRP * CH

                    def chain(slotname, j_, base):
                        return V(slotname, base + j_ * CH + 2, TT)

                    if s >= 1:
                        w = [("sH", g - 1)] if g >= 2 else []
                        op("scalar", lambda e, db=db, n0=n0, abase=abase: e.activation(
                            out=V("b", abase, NGRP * CH)
                            .rearrange("p (q c) -> p q c", c=CH)[:, :, 0:1].squeeze(2),
                            in_=hcar[:, db * NSTATE + n0: db * NSTATE + n0 + NGRP],
                            func=AF.Copy),
                            waits=w)

                    for j in range(NGRP):
                        n = n0 + j
                        w = [("sDT", dt_ready[(s, db)])]
                        if j == 0:
                            w.append(("sSCv", aready))
                            if g >= 2:
                                w.append(("sH", g - 1))
                        op("scalar", lambda e, db=db, n=n, j=j, abase=abase: e.activation(
                            out=chain("a", j, abase), in_=dtv(db),
                            func=AF.Exp, scale=A_sb[:, db * NSTATE + n: db * NSTATE + n + 1]),
                            waits=w, inc="sA")
                    a_val = CNT["sA"]
                    flush_carry()
                    w = [("sU", s * NDB + db + 1), ("sBC", bc_val)]
                    op("vector", lambda e, db=db, n0=n0, abase=abase: e.tensor_tensor(
                        V("b", abase, NGRP * CH).rearrange("p (q c) -> p q c", c=CH)[:, :, 2:CH],
                        uv(db).unsqueeze(1).broadcast_to((128, NGRP, TT)),
                        V("Bbc", n0 * TT, NGRP * TT).rearrange("p (q t) -> p q t", q=NGRP),
                        ALU.mult),
                        waits=w, inc="sB")
                    b_val = CNT["sB"]
                    w = [("sA", a_val), ("sB", b_val)]
                    if CNT["sCY"]:
                        w.append(("sCY", CNT["sCY"]))
                    op("vector", lambda e, abase=abase: e.tensor_tensor_scan(
                        out=V("h", 0, NGRP * CH),
                        data0=V("a", abase, NGRP * CH),
                        data1=V("b", abase, NGRP * CH),
                        initial=0.0, op0=ALU.mult, op1=ALU.add),
                        waits=w, inc="sH")
                    pend_carry.append((CNT["sH"], db, n0))
                    flush_z()
                    w = [("sBC", bc_val)]
                    if g >= 1 and p1cut >= 3:
                        w.append(("sY", g))
                    op("vector", lambda e, gi=gi: e.tensor_tensor(
                        V("hC").bitcast(BF16)[:, 0:NGRP * CH],
                        V("h", 0, NGRP * CH),
                        V("Cbc", gi * NGRP * CH, NGRP * CH),
                        ALU.mult),
                        waits=w, inc="sHC")
                    hc_val = CNT["sHC"]
                    for j in (range(NGRP) if p1cut >= 3 else []):
                        dpar = db % 2
                        w = [("sHC", hc_val)] if j == 0 else []
                        if j == 0 and gi == 0:
                            w.append(("sIN", inval))
                            zneed = s * NDB + db - 1
                            if zneed > 0:
                                w.append(("sZst", zneed))
                        op("tensor", lambda e, j=j, dpar=dpar, gi=gi: e.matmul(
                            out=y_ps[dpar].ap()[:, 0:TT],
                            lhsT=eyeb_sb[:],
                            rhs=V("hC").bitcast(BF16)[:, j * CH + 2: j * CH + 2 + TT],
                            start=(gi == 0 and j == 0), stop=(gi == NGPD - 1 and j == NGRP - 1)),
                            waits=w, inc=("sY" if j == NGRP - 1 else None))
                if p1cut < 3:
                    continue
                dpar = db % 2
                zpar = (s * NDB + db) % 2
                w = [("sY", s * GSUB + (db + 1) * NGPD), ("sSCv", d1ready),
                     ("sBCW", 16 * (s + 1))]
                if s * NDB + db >= 2:
                    w.append((f"sZW{zpar}", CNT[f"sZW{zpar}"]))
                zcnt = s * NDB + db + 1

                def mk_z(w=w, s=s, db=db, dpar=dpar, zpar=zpar, zcnt=zcnt):
                    op("vector", lambda e: e.scalar_tensor_tensor(
                        out=V("ztr", zpar * TT, TT), in0=xtv(db), scalar=D1_sb[:, db:db + 1],
                        in1=y_ps[dpar].ap()[:, 0:TT], op0=ALU.mult, op1=ALU.add),
                        waits=w, inc="sZst")
                    op("sync", lambda e: e.dma_start(
                        out=z_scr[db * 128:(db + 1) * 128, s * TT:(s + 1) * TT],
                        in_=V("ztr", zpar * TT, TT)),
                        waits=[("sZst", zcnt)], inc=f"sZW{zpar}", amt=16)
                pend_z.append(mk_z)
                z_count = zcnt

        flush_z()
        al_ps = [xp_ps[0], xp_ps[1], y_ps[0], y_ps[1]]
        b_final = CNT["sB"]
        h_final = CNT["sH"]
        a_final = CNT["sA"]
        op("vector", lambda e: e.memset(onrbv(), 1.0),
           waits=[("sB", CNT["sB"])], inc="sSCv")
        op("sync", lambda e: e.dma_start(out=outfv()[0:1, :], in_=fbias),
           waits=[("sB", CNT["sB"])], inc="sIN", amt=16)
        fb2 = CNT["sIN"]
        op("vector", lambda e: e.tensor_copy(fb16v(), outfv()[0:1, :]),
           waits=[("sSPl", CNT["sSPl"]), ("sIN", fb2), ("sB", CNT["sB"])], inc="sSCv")
        fbready = CNT["sSCv"]
        sgn_final = 0
        bw_val = 0

        k_alpha = int(os.environ.get("KDBG_ALPHA", "1"))
        if stage >= 1 and k_alpha:
            def absbuf(ib):
                return V("a", 0, DM) if ib % 2 == 0 else V("b", 0, DM)

            for ib in range(NDB):
                ipar = ib % 2
                if ib == 0:
                    w = [("sB", b_final)]
                elif ib >= 2:
                    w = [("sABS", CNT["sABS"] - 1)]
                else:
                    w = []
                wflv = op("sync", lambda e, ib=ib, ipar=ipar: e.dma_start(
                    out=V("u", ipar * DM, DM), in_=wfT[ib * 128:(ib + 1) * 128, :]),
                    waits=w, inc=f"sWFL{ipar}", amt=16)
                w = [(f"sWFL{ipar}", wflv)]
                if ib == 0:
                    w += [("sH", h_final), ("sA", a_final)]
                if ib >= 2:
                    w.append(("sALM", ib - 1))
                op("scalar", lambda e, ib=ib, ipar=ipar: e.activation(
                    out=absbuf(ib), in_=V("u", ipar * DM, DM), func=AF.Abs),
                    waits=w, inc="sABS")
                absv = CNT["sABS"]
                for og in range(4):
                    w = [("sABS", absv)] if og == 0 else []
                    if ib == 0 and og == 0:
                        w += [("sZst", z_count), ("sSCv", constready)]
                    op("tensor", lambda e, og=og, ib=ib: e.matmul(
                        out=al_ps[og].ap()[0:1, :],
                        lhsT=onc_sb[:], rhs=absbuf(ib)[:, og * 512:(og + 1) * 512],
                        start=(ib == 0), stop=(ib == NDB - 1)),
                        waits=w, inc=("sALM" if og == 3 else None))
            for og in range(4):
                op("vector", lambda e, og=og: e.tensor_scalar_mul(
                    outfv()[0:1, og * 512:(og + 1) * 512], al_ps[og].ap()[0:1, :], 1.0 / DM),
                    waits=[("sALM", NDB)] if og == 0 else [], inc="sALR")
            op("sync", lambda e: e.dma_start(out=al_scr, in_=outfv()[0:1, :]),
               waits=[("sALR", 4)], inc="sALW", amt=16)
            for og in range(4):
                op("sync", lambda e, og=og: e.dma_start(
                    out=outfv()[:, og * 512:(og + 1) * 512],
                    in_=al_scr[0:1, og * 512:(og + 1) * 512].broadcast_to((128, 512))),
                    waits=[("sALW", 16)] if og == 0 else [], inc="sALB", amt=16)
            for ib in range(NDB):
                ipar = ib % 2
                w = [("sSGN", CNT["sSGN"] - 2)] if ib >= 2 else [("sALM", NDB)]
                wflv = op("sync", lambda e, ib=ib, ipar=ipar: e.dma_start(
                    out=V("u", ipar * DM, DM), in_=wfT[ib * 128:(ib + 1) * 128, :]),
                    waits=w, inc=f"sWFL{ipar}", amt=16)
                w = [(f"sWFL{ipar}", wflv)]
                if ib == 0:
                    w.append(("sXP", CNT["sXP"]))
                op("vector", lambda e, ipar=ipar: e.tensor_scalar(
                    V("a", 0, DM), V("u", ipar * DM, DM), 0.0, None, ALU.is_ge),
                    waits=w, inc="sSGN")
                op("vector", lambda e: e.tensor_scalar(
                    V("a", 0, DM), V("a", 0, DM), 2.0, 1.0, ALU.mult, ALU.subtract))
                w = [("sALB", 64)]
                op("vector", lambda e, ib=ib: e.tensor_mul(
                    wunion[:, ib * E: ib * E + DM], V("a", 0, DM), outfv()),
                    waits=w, inc="sSGN")
            sgn_final = CNT["sSGN"]
            bw_val = op("vector", lambda e: e.tensor_copy(c1_sb[:, 0:1], onc_sb[:]), inc="sBW")

        run_p2 = stage >= 3
        if stage < 2 and p1cut < 3:
            if k_cs:
                for db in range(NDB):
                    op("gpsimd", lambda e, db=db: e.dma_start(
                        out=y_out[db * 128:(db + 1) * 128, 0:TT * nsub_eff],
                        in_=P_scr[db * 128:(db + 1) * 128, 0:TT * nsub_eff]),
                        waits=([("sPW0", CNT["sPW0"]), ("sPW1", CNT["sPW1"]),
                                ("sH", CNT["sH"]), ("sB", CNT["sB"]),
                                ("sBC", CNT["sBC"])] if db == 0 else []),
                        inc="sOW", amt=16)
            else:
                op("gpsimd", lambda e: e.dma_start(out=y_out[0:128, 0:NDB * NSTATE], in_=A_sb[:]),
                    waits=[("sSCv", constready), ("sXP", CNT["sXP"]), ("sSPa", CNT["sSPa"]),
                           ("sSPl", CNT["sSPl"]), ("sSPv", CNT["sSPv"]), ("sDT", CNT["sDT"]), ("sBC", CNT["sBC"]),
                           ("sCast", CNT["sCast"]), ("sX", CNT["sX"])], inc="sOW", amt=16)
            op("gpsimd", lambda e: e.nop(), waits=[("sOW", CNT["sOW"])])
        if stage < 2 and p1cut >= 3:
            for db in range(NDB):
                op("gpsimd", lambda e, db=db: e.dma_start(
                    out=y_out[db * 128:(db + 1) * 128, 0:TT * nsub_eff],
                    in_=z_scr[db * 128:(db + 1) * 128, 0:TT * nsub_eff]),
                    waits=([("sZW0", CNT["sZW0"]), ("sZW1", CNT["sZW1"])] if db == 0 else []),
                    inc="sOW", amt=16)
            op("gpsimd", lambda e: e.nop(), waits=[("sOW", CNT["sOW"])])
        if stage >= 2:
            for db in range(NDB):
                w = [("sZst", z_count)] if db == 0 else []
                op("scalar", lambda e, db=db: e.activation(
                    out=Pseg[:, db * NSTATE:(db + 1) * NSTATE],
                    in_=A_sb[:, db * NSTATE:(db + 1) * NSTATE],
                    func=AF.Exp, scale=Pcar[:, db:db + 1]),
                    waits=w, inc="sPSG")
            op("gpsimd", lambda e: e.dma_start(out=cc_in[:, 0:256], in_=Pseg),
               waits=[("sPSG", NDB), ("sZst", z_count)], inc="sCCW", amt=16)
            flush_carry()
            op("gpsimd", lambda e: e.dma_start(out=cc_in[:, 256:512], in_=hcar[:]),
               waits=[("sCY", CNT["sCY"])], inc="sCCW", amt=16)
            op("gpsimd", lambda e: e.collective_compute(
                "AllGather", ALU.bypass, replica_groups=[list(range(CORES))],
                ins=[cc_in.opt()], outs=[cc_out.opt()]),
                waits=[("sCCW", 32)], inc="sCG")
            op("gpsimd", lambda e: e.dma_start(
                out=V("Cbc", 0, CORES * 512).rearrange("p (r c) -> p r c", r=CORES),
                in_=cc_out.rearrange("(r p) c -> p r c", p=128)),
                waits=[("sCG", 1), ("sHC", CNT["sHC"])], inc="sCCR", amt=16)
            op("vector", lambda e: e.memset(hin, 0.0),
               waits=[("sCCR", 16), ("sPW0", CNT["sPW0"]), ("sPW1", CNT["sPW1"]),
                      ("sSPl", CNT["sSPl"]), ("sSCv", constready)], inc="sSCv")
            for r in range(CORES):
                w = [("sSCv", CNT["sSCv"])] if r == 0 else []
                op("vector", lambda e, r=r: e.tensor_scalar(
                    htmp, V("Cbc", r * 512, 256), 1.0, mask_sb[:, r:r + 1],
                    ALU.subtract, ALU.mult), waits=w)
                op("vector", lambda e: e.tensor_scalar_add(htmp, htmp, 1.0))
                op("vector", lambda e: e.tensor_mul(hin, hin, htmp))
                op("vector", lambda e, r=r: e.tensor_scalar_mul(
                    htmp, V("Cbc", r * 512 + 256, 256), mask_sb[:, r:r + 1]))
                op("vector", lambda e: e.tensor_add(hin, hin, htmp), inc="sCMB")
            cmb_val = CNT["sCMB"]

        if stage == 2:
            op("gpsimd", lambda e: e.dma_start(out=y_out, in_=z_scr),
                waits=[("sZW0", CNT["sZW0"]), ("sZW1", CNT["sZW1"]), ("sCMB", cmb_val)], inc="sOW", amt=16)
            op("gpsimd", lambda e: e.nop(), waits=[("sOW", CNT["sOW"])])
        for s in (range(nsub_eff) if run_p2 else []):
            w = [("sPW0", CNT["sPW0"]), ("sPW1", CNT["sPW1"])]
            if s == 0:
                w.append(("sZst", z_count))
            else:
                w.append(("sEA", 256 * s))
            op("sync", lambda e, s=s: e.dma_start(
                out=V("xt").rearrange("p (b t) -> p b t", b=NDB),
                in_=P_scr[:, s * TT:(s + 1) * TT].rearrange("(b p) t -> p b t", p=128)),
                waits=w, inc="sP2", amt=16)
            w = [("sZW0", CNT["sZW0"]), ("sZW1", CNT["sZW1"])]
            if s == 0:
                w.append(("sBW", bw_val))
            elif p2cut >= 2:
                w.append(("sX2", s))
            else:
                w.append(("sZ2d", 16 * s))
            op("sync", lambda e, s=s: e.dma_start(
                out=V("dt").rearrange("p (b t) -> p b t", b=NDB),
                in_=z_scr[:, s * TT:(s + 1) * TT].rearrange("(b p) t -> p b t", p=128)),
                waits=w, inc="sZL", amt=16)
            for n in range(NSTATE):
                w = []
                if n == 0:
                    w = [("sCMB", cmb_val)] if s == 0 else [("sECd", s * GSUB * NGRP)]
                    w.append(("sBCW", 16 * nsub_eff))
                op("sync", lambda e, n=n, s=s: e.dma_start(
                    out=V("Cbc", n * TT, TT),
                    in_=bc_scr[16 + n:17 + n, s * TT:(s + 1) * TT].broadcast_to((128, TT))),
                    waits=w, inc="sBC2", amt=16)
            bc2_val = CNT["sBC2"]

            for db in range(NDB):
                for gi in range(NGPD):
                    g = s * GSUB + db * NGPD + gi
                    gpar = g % 2
                    n0 = gi * NGRP
                    for j in range(NGRP):
                        n = n0 + j
                        w = [("sP2", 16 * (s + 1))] if j == 0 else []
                        if j == 0 and g == 0:
                            w.append(("sSGN", sgn_final))
                        if j == 0 and g >= 2:
                            w.append(("sECd", (g - 1) * NGRP))
                        op("scalar", lambda e, db=db, n=n, gpar=gpar, j=j: e.activation(
                            out=V("a", (gpar * NGRP + j) * TT, TT), in_=xtv(db),
                            func=AF.Exp, scale=A_sb[:, db * NSTATE + n: db * NSTATE + n + 1]),
                            waits=w, inc="sEA")
                    ea_val = CNT["sEA"]
                    for j in range(NGRP):
                        n = n0 + j
                        w = [("sEA", ea_val - NGRP + j + 1)]
                        if j == 0:
                            w.append(("sBC2", bc2_val))
                            if g <= 1:
                                w.append(("sY", CNT["sY"]))
                            if g >= 1:
                                w.append(("sY2", g))
                        op("vector", lambda e, db=db, n=n, gpar=gpar, j=j: e.scalar_tensor_tensor(
                            out=V("hC").bitcast(BF16)[:, j * TT:(j + 1) * TT],
                            in0=V("a", (gpar * NGRP + j) * TT, TT),
                            scalar=hin[:, db * NSTATE + n: db * NSTATE + n + 1],
                            in1=V("Cbc", n * TT, TT), op0=ALU.mult, op1=ALU.mult),
                            waits=w, inc="sECd")
                    ec_val = CNT["sECd"]
                    for j in range(NGRP):
                        dpar = db % 2
                        w = [("sECd", ec_val)] if j == 0 else []
                        if j == 0 and gi == 0:
                            zneed = s * NDB + db - 1
                            if zneed > 0:
                                w.append(("sZ2d", zneed))
                            if s == 0 and db <= 1:
                                w += [("sALM", NDB), ("sALR", 4)]
                        op("tensor", lambda e, j=j, dpar=dpar, gi=gi: e.matmul(
                            out=y_ps[dpar].ap()[:, 0:TT],
                            lhsT=eyeb_sb[:],
                            rhs=V("hC").bitcast(BF16)[:, j * TT:(j + 1) * TT],
                            start=(gi == 0 and j == 0), stop=(gi == NGPD - 1 and j == NGRP - 1)),
                            waits=w, inc=("sY2" if j == NGRP - 1 else None))
                dpar = db % 2
                zpar = db % 2
                w = [("sY2", s * GSUB + (db + 1) * NGPD), ("sZL", 16 * (s + 1))]
                if s == 0 and db == 0:
                    w += [("sWFL0", CNT["sWFL0"]), ("sWFL1", CNT["sWFL1"])]
                op("vector", lambda e, db=db, dpar=dpar: e.tensor_add(
                    uv(db), dtv(db), y_ps[dpar].ap()[:, 0:TT]),
                    waits=w, inc="sZ2d")
                if p2cut < 2:
                    continue
                w = []
                if CNT["sS2"] >= s * NDB + db - 1 and s * NDB + db >= 2:
                    w.append(("sS2", s * NDB + db - 1))
                op("vector", lambda e, db=db, zpar=zpar: e.tensor_mul(
                    V("ztr", zpar * TT, TT), uv(db), uv(db)), waits=w, inc="sSQ")
                sq_val = CNT["sSQ"]
                w = [("sZ2d", s * NDB + db + 1)]
                if db == 0:
                    if s == 0:
                        w.append(("sALR", 4))
                    elif p2cut >= 3:
                        w.append(("sZ2t", 8 * s))
                op("tensor", lambda e, db=db: e.matmul(
                    out=fn_ps[0].ap()[0:1, TT:2 * TT], lhsT=onc_sb[:], rhs=uv(db),
                    start=(db == 0), stop=(db == NDB - 1)),
                    waits=w, inc=("sST" if db == NDB - 1 else None))
                w2 = [("sSQ", sq_val)]
                if db == 0 and s > 0 and p2cut >= 3:
                    w2.append(("sZ2t", 8 * s))
                op("tensor", lambda e, db=db, zpar=zpar: e.matmul(
                    out=fn_ps[1].ap()[0:1, 0:TT], lhsT=onc_sb[:], rhs=V("ztr", zpar * TT, TT),
                    start=(db == 0), stop=(db == NDB - 1)),
                    waits=w2, inc="sS2")
                if db == NDB - 1:
                    op("tensor", lambda e: e.matmul(
                        out=fn_ps[1].ap()[0:1, TT:TT + 1], lhsT=onc_sb[:],
                        rhs=uv(db)[:, 0:1], start=True, stop=True), inc="sST")
            if p2cut < 2:
                continue
            st_val = CNT["sST"]
            op("scalar", lambda e: e.activation(
                out=stv(0), in_=fn_ps[0].ap()[0:1, TT:2 * TT], func=AF.Copy,
                scale=1.0 / DM),
                waits=[("sST", st_val)], inc="sSCa")
            op("scalar", lambda e: e.activation(
                out=stv(1), in_=fn_ps[1].ap()[0:1, 0:TT], func=AF.Copy,
                scale=1.0 / DM))
            op("scalar", lambda e: e.activation(
                out=stv(2), in_=stv(0), func=AF.Square),
                inc="sSCa")
            sca_val = CNT["sSCa"]
            op("vector", lambda e: e.tensor_sub(
                stv(1), stv(1),
                stv(2)),
                waits=[("sSCa", sca_val)], inc="sSCv")
            scv_val = CNT["sSCv"]
            op("scalar", lambda e: e.activation(
                out=stv(2), in_=stv(1),
                func=AF.Ln, bias=eps_sb[0:1, 0:1]),
                waits=[("sSCv", scv_val)])
            op("scalar", lambda e: e.activation(
                out=stv(1), in_=stv(2),
                func=AF.Exp, scale=-0.5),
                inc="sRST")
            rst_val = CNT["sRST"]
            w = [("sRST", rst_val)]
            if s > 0:
                w.append(("sMUB", 32 * s))
            op("scalar", lambda e: e.dma_start(out=st_scr[0:1, :], in_=stv(0)),
               waits=w, inc="sSTW", amt=16)
            op("scalar", lambda e: e.dma_start(out=st_scr[1:2, :], in_=stv(1)),
               inc="sSTW", amt=16)
            op("sync", lambda e: e.dma_start(
                out=mubv(), in_=st_scr[0:1, :].broadcast_to((128, TT))),
                waits=[("sSTW", 32 * (s + 1))], inc="sMUB", amt=16)
            op("sync", lambda e: e.dma_start(
                out=rbv(), in_=st_scr[1:2, :].broadcast_to((128, TT))),
                inc="sMUB", amt=16)
            mub_val = CNT["sMUB"]
            w = [("sMUB", mub_val)]
            if s == 0:
                w.append(("sB", b_final))
            elif p2cut >= 3 and CNT["sOF"] >= 2 * s:
                w.append(("sOF", 2 * s))
            elif p2cut >= 3:
                w.append(("sOW", 16 * 2 * s))
            op("vector", lambda e: e.tensor_tensor(
                V("dt").rearrange("p (b t) -> p b t", b=NDB),
                V("u").rearrange("p (b t) -> p b t", b=NDB),
                mubv().unsqueeze(1).broadcast_to((128, NDB, TT)), ALU.subtract),
                waits=w)
            op("vector", lambda e: e.tensor_tensor(
                x2bv().rearrange("p (b t) -> p b t", b=NDB),
                V("dt").rearrange("p (b t) -> p b t", b=NDB),
                rbv().unsqueeze(1).broadcast_to((128, NDB, TT)), ALU.mult),
                inc="sX2")
            x2_val = CNT["sX2"]

            for tb in (range(2) if p2cut >= 3 else []):
                for og in range(4):
                    fpar = og % 2
                    for ib in range(NDB):
                        w = []
                        if ib == 0:
                            w += [("sBW", bw_val), ("sSCv", fbready),
                                  ("sX2", x2_val)]
                            prev = s * 8 + tb * 4 + og - 1
                            if prev > 0:
                                w.append(("sZ2t", prev))
                        nobias = int(os.environ.get("KDBG_NOBIAS", "0"))
                        op("tensor", lambda e, og=og, ib=ib, tb=tb, fpar=fpar,
                           st=bool(nobias and ib == NDB - 1): e.matmul(
                            out=fn_ps[fpar].ap(),
                            lhsT=x2bv()[:, ib * TT + tb * 128: ib * TT + tb * 128 + 128],
                            rhs=wunion[:, ib * E + og * 512: ib * E + og * 512 + 512],
                            start=(ib == 0), stop=st),
                            waits=w)
                    for k in (range(4) if not int(os.environ.get("KDBG_NOTR", "0")) else []):
                        ibf = og * 4 + k
                        w = [("sIN", inval)] if k == 0 else []
                        op("tensor", lambda e, ibf=ibf, tb=tb, k=k, fpar=fpar: e.matmul(
                            out=fn_ps[fpar].ap()[:, k * 128:(k + 1) * 128],
                            lhsT=x2bv()[:, ibf * TT + tb * 128: ibf * TT + tb * 128 + 128],
                            rhs=eyeb_sb[:],
                            start=False, stop=False),
                            waits=w)
                    if not int(os.environ.get("KDBG_NOBIAS", "0")):
                        op("tensor", lambda e, og=og, fpar=fpar: e.matmul(
                            out=fn_ps[fpar].ap(), lhsT=onrbv(),
                            rhs=fb16v()[0:1, og * 512:(og + 1) * 512],
                            start=False, stop=True),
                            inc="sFM")
                    else:
                        op("tensor", lambda e: e.nop(), inc="sFM")
                    fm2 = CNT["sFM"]
                    op("vector", lambda e, og=og, tb=tb, fpar=fpar: e.tensor_copy(
                        V("Bbc", tb * DM + og * 512, 512),
                        fn_ps[fpar].ap()),
                        waits=[("sFM", fm2)], inc="sZ2t")
                if int(os.environ.get("KDBG_NOLN2", "0")):
                    w = [("sZ2t", CNT["sZ2t"])]
                    tcount = s * 2 + tb
                    if tcount > 0:
                        w.append(("sOW", 16 * tcount))
                    op("scalar", lambda e, s=s, tb=tb: e.dma_start(
                        out=y_out[s * TT + tb * 128: s * TT + (tb + 1) * 128, :],
                        in_=V("Bbc", tb * DM, DM)), waits=w, inc="sOW", amt=16)
                    continue
                ln2cut = int(os.environ.get("KDBG_LN2CUT", "3"))
                op("vector", lambda e, tb=tb: e.tensor_scalar(
                    V("u", tb * DM, DM), V("Bbc", tb * DM, DM), 1.0, None,
                    ALU.mult, ALU.add, accum_out=c1_sb[:, 0:1]),
                    waits=[("sZ2t", CNT["sZ2t"])])
                op("vector", lambda e, tb=tb: e.scalar_tensor_tensor(
                    out=V("u", tb * DM, DM), in0=V("Bbc", tb * DM, DM), scalar=1.0,
                    in1=V("Bbc", tb * DM, DM), op0=ALU.mult, op1=ALU.mult,
                    accum_out=c1_sb[:, 1:2]))
                op("vector", lambda e: e.tensor_scalar_mul(c1_sb[:, 2:3], c1_sb[:, 0:1], 1.0 / DM))
                op("vector", lambda e: e.tensor_scalar_mul(c1_sb[:, 3:4], c1_sb[:, 1:2], 1.0 / DM),
                   inc="sSCv")
                scv2 = CNT["sSCv"]
                if ln2cut < 2:
                    w = [("sSCv", scv2), ("sZ2t", CNT["sZ2t"])]
                    tcount = s * 2 + tb
                    if tcount > 0:
                        w.append(("sOW", 16 * tcount))
                    op("scalar", lambda e, s=s, tb=tb: e.dma_start(
                        out=y_out[s * TT + tb * 128: s * TT + (tb + 1) * 128, :],
                        in_=V("Bbc", tb * DM, DM)), waits=w, inc="sOW", amt=16)
                    continue
                op("scalar", lambda e: e.activation(
                    out=c1_sb[:, 4:5], in_=c1_sb[:, 2:3], func=AF.Square),
                    waits=[("sSCv", scv2)], inc="sSCa")
                op("vector", lambda e: e.tensor_sub(c1_sb[:, 5:6], c1_sb[:, 3:4], c1_sb[:, 4:5]),
                   waits=[("sSCa", CNT["sSCa"])], inc="sSCv")
                op("scalar", lambda e: e.activation(
                    out=c1_sb[:, 6:7], in_=c1_sb[:, 5:6], func=AF.Ln, bias=eps_sb[:, 0:1]),
                    waits=[("sSCv", CNT["sSCv"])])
                op("scalar", lambda e: e.activation(
                    out=c1_sb[:, 7:8], in_=c1_sb[:, 6:7], func=AF.Exp, scale=-0.5),
                    inc="sR2")
                r2_val = CNT["sR2"]
                if ln2cut < 3:
                    w = [("sR2", r2_val), ("sZ2t", CNT["sZ2t"])]
                    tcount = s * 2 + tb
                    if tcount > 0:
                        w.append(("sOW", 16 * tcount))
                    op("scalar", lambda e, s=s, tb=tb: e.dma_start(
                        out=y_out[s * TT + tb * 128: s * TT + (tb + 1) * 128, :],
                        in_=V("Bbc", tb * DM, DM)), waits=w, inc="sOW", amt=16)
                    continue
                w = [("sR2", r2_val)]
                tcount = s * 2 + tb
                if tcount > 0:
                    w.append(("sOW", 16 * tcount))
                op("vector", lambda e, tb=tb: e.tensor_scalar(
                    outfv(), V("Bbc", tb * DM, DM), c1_sb[:, 2:3], c1_sb[:, 7:8],
                    ALU.subtract, ALU.mult),
                    waits=w, inc="sOF")
                of_val = CNT["sOF"]
                op("scalar", lambda e, s=s, tb=tb: e.dma_start(
                    out=y_out[s * TT + tb * 128: s * TT + (tb + 1) * 128, :],
                    in_=outfv()),
                    waits=[("sOF", of_val)], inc="sOW", amt=16)

        if run_p2 and p2cut < 3:
            w = [("sZ2d", CNT["sZ2d"]), ("sX2", CNT["sX2"]), ("sEA", CNT["sEA"]),
                 ("sECd", CNT["sECd"]), ("sSQ", CNT["sSQ"]), ("sST", CNT["sST"]),
                 ("sMUB", CNT["sMUB"])]
            op("gpsimd", lambda e: e.dma_start(out=y_out[0:128, 0:NDB], in_=D1_sb[:]),
                waits=w, inc="sOW", amt=16)
            op("gpsimd", lambda e: e.nop(), waits=[("sOW", CNT["sOW"])])
        if run_p2 and p2cut >= 3:
            final_ow = CNT["sOW"]
            op("gpsimd", lambda e: e.nop(), waits=[("sOW", final_ow)])

        with nc.Block() as block:
            def runner(name):
                def _run(eng):
                    for fn, waits, inc, amt in PROG[name]:
                        for sem, v in waits:
                            eng.wait_ge(sem, v)
                        ins = fn(eng)
                        if inc is not None:
                            ins.then_inc(inc, amt)
                return _run

            block.sync(runner("sync"))
            block.scalar(runner("scalar"))
            block.vector(runner("vector"))
            block.tensor(runner("tensor"))
            block.gpsimd(runner("gpsimd"))
    return nc


_NC_CACHE = None


def kernel(L0_attn_x, L0_attn_A_log, L0_attn_D, L0_attn_W_xproj, L0_ffn_weight, L0_ffn_bias):
    global _NC_CACHE
    x2d = np.asarray(L0_attn_x, np.float32).reshape(L, DM)
    wxT = np.ascontiguousarray(np.asarray(L0_attn_W_xproj, np.float32).T)
    wfT = np.ascontiguousarray(np.asarray(L0_ffn_weight, np.float32).T)
    alogr = np.ascontiguousarray(
        np.asarray(L0_attn_A_log, np.float32).reshape(NDB, 128, NSTATE).transpose(1, 0, 2)
    ).reshape(128, NDB * NSTATE)
    dvecr = np.ascontiguousarray(np.asarray(L0_attn_D, np.float32).reshape(NDB, 128).T)
    fbias = np.asarray(L0_ffn_bias, np.float32).reshape(1, DM)
    import ml_dtypes
    eye = np.eye(128, dtype=ml_dtypes.bfloat16)

    if _NC_CACHE is None:
        _NC_CACHE = build_nc()
    nc = _NC_CACHE

    in_maps = []
    for k in range(CORES):
        xTk = np.ascontiguousarray(x2d[k * SL:(k + 1) * SL].T)
        cm = np.zeros((128, CORES), np.float32)
        cm[:, :k] = 1.0
        in_maps.append({
            "xT": xTk, "wxT": wxT, "wfT": wfT, "alogr": alogr, "dvecr": dvecr,
            "fbias": fbias, "eye_in": eye, "cmask": cm,
        })
    res = run_bass_kernel_spmd(nc, in_maps, core_ids=list(range(CORES)))
    out = np.concatenate([r["y_out"] for r in res.results], axis=0)
    return out.reshape(1, L, DM)
